# version 01 (final)
# baseline (speedup 1.0000x reference)
import contextlib
import numpy as np

import concourse.bass as bass
import concourse.mybir as mybir
from concourse.bass_utils import run_bass_kernel_spmd

F32 = mybir.dt.float32
BF16 = mybir.dt.bfloat16
I32 = mybir.dt.int32
AF = mybir.ActivationFunctionType
ALU = mybir.AluOpType
AX = mybir.AxisListType.X

NCORE = 8
T = 2048
H = 4096
DIN = 8192
NST = 16
RNK = 256
ECNT = RNK + 2 * NST
NT = T // 128
NH = H // 128
NJ = 2 * DIN // 128
NJX = DIN // 128
ND = DIN // 128
NO = H // 128
LC = 512
NCH = T // LC
EPS_LN = 1e-5
EPS_RMS = 1e-6
NORM_STEPS = 23


def build_l1(debug_xn=False):
    nc = bass.Bass("TRN2", target_bir_lowering=False, detect_race_conditions=False)

    x_in = nc.dram_tensor("x", [T, H], F32, kind="ExternalInput").ap()
    xh_in = nc.dram_tensor("xhalo", [3, H], F32, kind="ExternalInput").ap()
    wp_in = nc.dram_tensor("wp", [H, 2 * DIN], F32, kind="ExternalInput").ap()
    xpw_in = nc.dram_tensor("xpw", [DIN, ECNT], F32, kind="ExternalInput").ap()
    dtw_in = nc.dram_tensor("dtw", [RNK, DIN], F32, kind="ExternalInput").ap()
    fw_in = nc.dram_tensor("fw", [H, H], F32, kind="ExternalInput").ap()
    cent_in = nc.dram_tensor("cent", [H, 64], F32, kind="ExternalInput").ap()
    alog_in = nc.dram_tensor("alog", [128, ND * NST], F32, kind="ExternalInput").ap()
    convw_in = nc.dram_tensor("convw", [128, NJX * 4], F32, kind="ExternalInput").ap()
    convb_in = nc.dram_tensor("convb", [128, NJX], F32, kind="ExternalInput").ap()
    dtbias_in = nc.dram_tensor("dtbias", [128, ND], F32, kind="ExternalInput").ap()
    dtln_in = nc.dram_tensor("dtln", [128, 2], F32, kind="ExternalInput").ap()
    blnw_in = nc.dram_tensor("blnw", [NST, 1], F32, kind="ExternalInput").ap()
    clnw_in = nc.dram_tensor("clnw", [NST, 1], F32, kind="ExternalInput").ap()
    fbias_in = nc.dram_tensor("fbias", [128, NO], F32, kind="ExternalInput").ap()
    ident_in = nc.dram_tensor("ident", [128, 128], F32, kind="ExternalInput").ap()
    onesc_in = nc.dram_tensor("onesc", [128, 1], F32, kind="ExternalInput").ap()
    onesr_in = nc.dram_tensor("onesr", [1, 128], F32, kind="ExternalInput").ap()
    ones16c_in = nc.dram_tensor("ones16c", [NST, 1], F32, kind="ExternalInput").ap()
    sel16_in = nc.dram_tensor("sel16", [NST, NST * 128], F32, kind="ExternalInput").ap()
    fconst_in = nc.dram_tensor("fconst", [128, 4], F32, kind="ExternalInput").ap()

    xc_sp = nc.dram_tensor("xc_sp", [DIN, T], F32, kind="ExternalOutput").ap()
    sz_sp = nc.dram_tensor("sz_sp", [DIN, T], F32, kind="ExternalOutput").ap()
    dt_sp = nc.dram_tensor("dt_sp", [DIN, T], F32, kind="ExternalOutput").ap()
    y0_sp = nc.dram_tensor("y0_sp", [DIN, T], F32, kind="ExternalOutput").ap()
    dn_sp = nc.dram_tensor("dn_sp", [H, T], F32, kind="ExternalOutput").ap()
    cv_out = nc.dram_tensor("cv", [NST, T], F32, kind="ExternalOutput").ap()
    qselt_out = nc.dram_tensor("qselt", [NT, 128], F32, kind="ExternalOutput").ap()
    hits_out = nc.dram_tensor("hits", [1, 64], F32, kind="ExternalOutput").ap()
    qc_out = nc.dram_tensor("qc", [128, ND * NST], F32, kind="ExternalOutput").ap()
    pc_out = nc.dram_tensor("pc", [128, ND * NST], F32, kind="ExternalOutput").ap()
    xnd_out = (nc.dram_tensor("xnd", [128, NT * H], mybir.dt.bfloat16, kind="ExternalOutput").ap()
               if debug_xn else None)

    NCONST = 13

    es = contextlib.ExitStack()
    with es:
        sb = lambda name, shape, dt=F32: es.enter_context(nc.sbuf_tensor(name, shape, dt))
        sem = lambda name: es.enter_context(nc.semaphore(name))

        xnh_bf = sb("xnh_bf", [128, NH * 3], BF16)
        ident_sb = sb("ident_sb", [128, 128])
        onesc_sb = sb("onesc_sb", [128, 1])
        onesr_sb = sb("onesr_sb", [1, 128])
        ones16c_sb = sb("ones16c_sb", [NST, 1])
        an_sb = sb("an_sb", [128, ND * NST])
        aexp_sb = sb("aexp_sb", [128, ND * NST])
        convw_sb = sb("convw_sb", [128, NJX * 4])
        convb_sb = sb("convb_sb", [128, NJX])
        dtbias_sb = sb("dtbias_sb", [128, ND])
        dtln_sb = sb("dtln_sb", [128, 2])
        blnw_sb = sb("blnw_sb", [NST, 1])
        clnw_sb = sb("clnw_sb", [NST, 1])
        fbias_sb = sb("fbias_sb", [128, NO])
        cdtot_sb = sb("cdtot_sb", [128, ND])
        state_sb = sb("state_sb", [128, ND * NST])
        qsel_cols = sb("qsel_cols", [128, NT])
        hs_sb = sb("hs_sb", [1, 64])
        qt_sb = sb("qt_sb", [NT, 128])
        col_sb = sb("col_sb", [128, 8])
        fconst_sb = sb("fconst_sb", [128, 4])
        nc.const_aps.register(F32, EPS_LN, fconst_sb[:, 0:1])
        nc.const_aps.register(F32, EPS_RMS, fconst_sb[:, 1:2])
        nc.const_aps.register(F32, 1.0, fconst_sb[:, 2:3])

        pA = es.enter_context(nc.psum_tensor("pA", [128, 1024], F32))
        pB = es.enter_context(nc.psum_tensor("pB", [128, 1024], F32))
        pC = es.enter_context(nc.psum_tensor("pC", [128, 2048], F32))

        ld = sem("ld"); stg = sem("stg"); s0 = sem("s0")
        v_xn = sem("v_xn"); s_rst = sem("s_rst"); v_sq = sem("v_sq")
        t_tp = sem("t_tp"); s_ev = sem("s_ev"); t_dm = sem("t_dm")
        v_rm = sem("v_rm"); s_sm = sem("s_sm"); v_ht = sem("v_ht"); t_hs = sem("t_hs")
        w_ld = sem("w_ld"); v_cst = sem("v_cst"); t_mm = sem("t_mm"); s_evx = sem("s_evx")
        t_hl = sem("t_hl"); s_hl = sem("s_hl"); v_cv = sem("v_cv"); s_xc = sem("s_xc")
        g_xc = sem("g_xc"); g_sz = sem("g_sz")
        w3_ld = sem("w3_ld"); v3_cst = sem("v3_cst"); t3_mm = sem("t3_mm"); s3_ev = sem("s3_ev")
        g_dn = sem("g_dn")
        x2_ld = sem("x2_ld"); x2e = sem("x2e"); x2o = sem("x2o")
        v_xc2 = sem("v_xc2"); t_sp = sem("t_sp"); nseq = sem("nseq")
        t_dt = sem("t_dt"); s_sp = sem("s_sp"); v_dt = sem("v_dt"); g_dt = sem("g_dt")
        t_br = sem("t_br"); s_br = sem("s_br"); p1_ld = sem("p1_ld")
        p1e = sem("p1e"); p1o = sem("p1o")
        s_da = sem("s_da"); v_sc = sem("v_sc"); v_y0 = sem("v_y0"); g_y0 = sem("g_y0")
        v_st = sem("v_st")
        v_pc = sem("v_pc"); s_pc = sem("s_pc"); g_fin = sem("g_fin")
        s_self = sem("s_self")
        SSELF = [0]
        def act_sync(e, inst):
            SSELF[0] += 1
            inst.then_inc(s_self, 1)
            e.wait_ge(s_self, SSELF[0])
        v_self = sem("v_self")
        VSELF = [0]
        def dve_sync(e, inst):
            VSELF[0] += 1
            inst.then_inc(v_self, 1)
            e.wait_ge(v_self, VSELF[0])

        block = es.enter_context(nc.Block())
        STG = [0]

        def barrier():
            STG[0] += 1
            k = STG[0]
            for emit in (block.sync, block.vector, block.scalar, block.tensor, block.gpsimd):
                def _f(e, k=k):
                    e.drain().then_inc(stg, 1)
                    e.wait_ge(stg, 5 * k)
                emit(_f)

        def xn_rhs(xn_bf, ht, lsub):
            ap4 = xn_bf.ap().rearrange("p (t h l) -> p t h l", t=NT, h=NH)
            return ap4[:, lsub * 4:(lsub + 1) * 4, ht:ht + 1, :]

        def st0_sync(e):
            e.dma_start(out=alog_sb[:], in_=alog_in).then_inc(ld, 16)
            e.dma_start(out=convw_sb[:], in_=convw_in).then_inc(ld, 16)
            e.dma_start(out=convb_sb[:], in_=convb_in).then_inc(ld, 16)
            e.dma_start(out=dtbias_sb[:], in_=dtbias_in).then_inc(ld, 16)
            e.dma_start(out=dtln_sb[:], in_=dtln_in).then_inc(ld, 16)
            e.dma_start(out=blnw_sb[:], in_=blnw_in).then_inc(ld, 16)
            e.dma_start(out=clnw_sb[:], in_=clnw_in).then_inc(ld, 16)
            e.dma_start(out=fbias_sb[:], in_=fbias_in).then_inc(ld, 16)
            e.dma_start(out=ident_sb[:], in_=ident_in).then_inc(ld, 16)
            e.dma_start(out=onesc_sb[:], in_=onesc_in).then_inc(ld, 16)
            e.dma_start(out=onesr_sb[:], in_=onesr_in).then_inc(ld, 16)
            e.dma_start(out=ones16c_sb[:], in_=ones16c_in).then_inc(ld, 16)
            e.dma_start(out=fconst_sb[:], in_=fconst_in).then_inc(ld, 16)

        def st0_scalar(e):
            e.wait_ge(ld, 16 * NCONST)
            e.activation(out=aexp_sb[:], in_=alog_sb[:], func=AF.Exp).then_inc(s0, 1)

        def st0_vector(e):
            e.wait_ge(s0, 1)
            e.tensor_scalar(out=an_sb[:], in0=aexp_sb[:], scalar1=-1.0, scalar2=None, op0=ALU.mult)
            e.memset(state_sb[:], 0.0)

        TP0 = 4
        EV0 = 4
        tp_after = lambda t, g: TP0 + 4 * t + g + 1
        ev_after = lambda t, g: EV0 + 8 * t + 2 * (g + 1)
        dm_after = lambda t, g: 4 * t + g + 1

        def st1_sync(e):
            e.dma_start(out=cent_sb.ap().rearrange("p (ht c) -> p ht c", ht=NH),
                        in_=cent_in.rearrange("(ht p) c -> p ht c", p=128)).then_inc(ld, 16)
            e.dma_start(out=xb_sb[0:3, :], in_=xh_in).then_inc(ld, 16)
            for t in range(NT):
                e.wait_ge(s_rst, t + 1)
                e.dma_start(out=xb_sb[:], in_=x_in[t * 128:(t + 1) * 128, :]).then_inc(ld, 16)

        def st1_vector(e):
            e.wait_ge(ld, 16 * (NCONST + 2))
            dve_sync(e, e.reduce_sum(out=col_sb[0:3, 0:1], in_=xb_sb[0:3, :], axis=AX))
            dve_sync(e, e.tensor_scalar(out=col_sb[0:3, 1:2], in0=col_sb[0:3, 0:1], scalar1=1.0 / H,
                                        scalar2=None, op0=ALU.mult))
            e.tensor_scalar(out=xw_sb[0:3, :], in0=xb_sb[0:3, :], scalar1=col_sb[0:3, 1:2],
                            scalar2=None, op0=ALU.subtract).then_inc(v_sq, 1)
            e.wait_ge(s_rst, 1)
            e.tensor_scalar(out=xw_sb[0:3, :], in0=xw_sb[0:3, :], scalar1=col_sb[0:3, 2:3],
                            scalar2=None, op0=ALU.mult).then_inc(v_xn, 1)
            for t in range(NT):
                e.wait_ge(ld, 16 * (NCONST + t + 3))
                dve_sync(e, e.reduce_sum(out=col_sb[:, 0:1], in_=xb_sb[:], axis=AX))
                dve_sync(e, e.tensor_scalar(out=col_sb[:, 1:2], in0=col_sb[:, 0:1], scalar1=1.0 / H,
                                            scalar2=None, op0=ALU.mult))
                e.wait_ge(t_tp, 4 if t == 0 else tp_after(t - 1, 3))
                e.tensor_scalar(out=xw_sb[:], in0=xb_sb[:], scalar1=col_sb[:, 1:2],
                                scalar2=None, op0=ALU.subtract).then_inc(v_sq, 1)
                e.wait_ge(s_rst, t + 2)
                e.tensor_scalar(out=xw_sb[:], in0=xw_sb[:], scalar1=col_sb[:, 2:3],
                                scalar2=None, op0=ALU.mult).then_inc(v_xn, 1)
                e.wait_ge(t_dm, dm_after(t, 3))
                if t >= 1:
                    e.wait_ge(s_sm, t)
                dve_sync(e, e.tensor_scalar(out=lgs_sb[:], in0=pC[:, 0:64], scalar1=10.0,
                                            scalar2=None, op0=ALU.mult))
                dve_sync(e, e.reduce_max(out=col_sb[:, 3:4], in_=lgs_sb[:], axis=AX))
                e.tensor_scalar(out=col_sb[:, 4:5], in0=col_sb[:, 3:4], scalar1=-1.0,
                                scalar2=None, op0=ALU.mult).then_inc(v_rm, 1)
                e.wait_ge(s_sm, t + 1)
                dve_sync(e, e.reduce_sum(out=col_sb[:, 5:6], in_=e_sb[:], axis=AX))
                dve_sync(e, e.reciprocal(out=col_sb[:, 6:7], in_=col_sb[:, 5:6]))
                dve_sync(e, e.tensor_scalar(out=dots_sb[:], in0=e_sb[:], scalar1=col_sb[:, 6:7],
                                            scalar2=None, op0=ALU.mult))
                if t >= 2:
                    e.wait_ge(t_hs, t - 1)
                e.tensor_scalar(out=hits_sb[t % 2][:], in0=dots_sb[:], scalar1=0.01,
                                scalar2=None, op0=ALU.is_gt).then_inc(v_ht, 1)
                dve_sync(e, e.reduce_max(out=col_sb[:, 3:4], in_=dots_sb[:], axis=AX))
                e.tensor_scalar(out=qsel_cols[:, t:t + 1], in0=col_sb[:, 3:4], scalar1=0.01,
                                scalar2=None, op0=ALU.is_gt).then_inc(v_ht, 1)

        def st1_scalar(e):
            e.wait_ge(v_sq, 1)
            act_sync(e, e.activation(out=xb_sb[0:3, :], in_=xw_sb[0:3, :], func=AF.Square,
                                     accum_out=col_sb[0:3, 7:8]))
            act_sync(e, e.activation(out=col_sb[0:3, 2:3], in_=col_sb[0:3, 7:8], func=AF.Ln,
                                     scale=1.0 / H, bias=EPS_LN))
            e.activation(out=col_sb[0:3, 2:3], in_=col_sb[0:3, 2:3], func=AF.Exp, scale=-0.5).then_inc(s_rst, 1)
            for g in range(4):
                e.wait_ge(t_tp, g + 1)
                e.activation(out=xnh_bf[:, g * 24:(g + 1) * 24],
                             in_=(pA if g % 2 == 0 else pB)[:, 0:24], func=AF.Copy).then_inc(s_ev, 1)
            for t in range(NT):
                e.wait_ge(v_sq, t + 2)
                act_sync(e, e.activation(out=xb_sb[:], in_=xw_sb[:], func=AF.Square,
                                         accum_out=col_sb[:, 7:8]))
                act_sync(e, e.activation(out=col_sb[:, 2:3], in_=col_sb[:, 7:8], func=AF.Ln,
                                         scale=1.0 / H, bias=EPS_LN))
                e.activation(out=col_sb[:, 2:3], in_=col_sb[:, 2:3], func=AF.Exp, scale=-0.5).then_inc(s_rst, 1)
                for g in range(4):
                    e.wait_ge(t_tp, tp_after(t, g))
                    if 4 * t + g - 1 > 0:
                        e.wait_ge(t_dm, 4 * t + g - 1)
                    src = pA if g % 2 == 0 else pB
                    e.activation(out=xnf_sb[g % 2][:], in_=src[:], func=AF.Copy).then_inc(s_ev, 1)
                    e.activation(out=xn_bf[:, t * H + g * 1024:t * H + (g + 1) * 1024],
                                 in_=src[:], func=AF.Copy).then_inc(s_ev, 1)
                e.wait_ge(v_rm, t + 1)
                if t >= 1:
                    e.wait_ge(v_ht, 2 * t)
                e.activation(out=e_sb[:], in_=lgs_sb[:], func=AF.Exp,
                             bias=col_sb[:, 4:5]).then_inc(s_sm, 1)

        def st1_tensor(e):
            e.wait_ge(ld, 16 * (NCONST + 2))
            e.wait_ge(v_xn, 1)
            for g in range(4):
                dst = pA if g % 2 == 0 else pB
                if g >= 2:
                    e.wait_ge(s_ev, g - 1)
                for b in range(8):
                    ht = g * 8 + b
                    mm = e.transpose(out=dst[:, b * 3:(b + 1) * 3],
                                     in_=xw_sb[0:3, ht * 128:(ht + 1) * 128],
                                     identity=ident_sb[0:3, 0:3])
                mm.then_inc(t_tp, 1)
            for t in range(NT):
                e.wait_ge(v_xn, t + 2)
                for g in range(4):
                    dst = pA if g % 2 == 0 else pB
                    if t == 0 and g < 2:
                        e.wait_ge(s_ev, g + 3)
                    else:
                        e.wait_ge(s_ev, ev_after(t, g) - 4)
                    for b in range(8):
                        ht = g * 8 + b
                        mm = e.transpose(out=dst[:, b * 128:(b + 1) * 128],
                                         in_=xw_sb[:, ht * 128:(ht + 1) * 128],
                                         identity=ident_sb[:])
                    mm.then_inc(t_tp, 1)
                if t >= 1:
                    e.wait_ge(v_rm, t)
                    e.wait_ge(s_sm, t)
                for g in range(4):
                    e.wait_ge(s_ev, ev_after(t, g) - 1)
                    for b in range(8):
                        ht = g * 8 + b
                        mm = e.matmul(pC[:, 0:64], xnf_sb[g % 2][:, b * 128:(b + 1) * 128],
                                      cent_sb[:, ht * 64:(ht + 1) * 64],
                                      start=(ht == 0), stop=(ht == NH - 1))
                    mm.then_inc(t_dm, 1)
                if t >= 1:
                    e.wait_ge(v_ht, 2 * (t - 1) + 1)
                    e.matmul(pC[0:1, 512:576], onesc_sb[:], hits_sb[(t - 1) % 2][:],
                             start=(t - 1 == 0), stop=False).then_inc(t_hs, 1)
            e.wait_ge(v_ht, 2 * (NT - 1) + 1)
            e.matmul(pC[0:1, 512:576], onesc_sb[:], hits_sb[(NT - 1) % 2][:],
                     start=False, stop=True).then_inc(t_hs, 1)
            e.wait_ge(v_ht, 2 * NT)
            e.transpose(out=pC[0:16, 1024:1152], in_=qsel_cols[:], identity=ident_sb[:]).then_inc(t_hs, 1)

        def st1_scalar_tail(e):
            e.wait_ge(t_hs, NT + 1)
            e.activation(out=hs_sb[:], in_=pC[0:1, 512:576], func=AF.Copy).then_inc(s_sm, 1)
            e.activation(out=qt_sb[:], in_=pC[0:16, 1024:1152], func=AF.Copy).then_inc(s_sm, 1)

        def st1_gpsimd(e):
            if debug_xn:
                e.wait_ge(s_ev, ev_after(NT - 1, 3))
                e.dma_start(out=xnd_out, in_=xn_bf[:]).then_inc(g_fin, 16)
            e.wait_ge(s_sm, NT + 2)
            e.dma_start(out=hits_out, in_=hs_sb[:]).then_inc(g_fin, 16)
            e.dma_start(out=qselt_out, in_=qt_sb[:]).then_inc(g_fin, 16)

        def st2_sync(e):
            for j in range(NJ):
                for hf in range(2):
                    if 2 * j + hf >= 1:
                        e.wait_ge(v_cst, 2 * j + hf)
                    e.dma_start(out=wst_sb.ap().rearrange("p (ht j) -> p ht j", ht=NH // 2),
                                in_=wp_in[hf * 2048:(hf + 1) * 2048, j * 128:(j + 1) * 128]
                                .rearrange("(ht p) j -> p ht j", p=128)).then_inc(w_ld, 16)

        def st2_vector(e):
            for j in range(NJ):
                for hf in range(2):
                    e.wait_ge(w_ld, 16 * (2 * j + hf + 1))
                    if j >= 2 and hf == 0:
                        e.wait_ge(t_mm, 4 * (j - 1))
                    e.tensor_copy(out=wbf_sb[j % 2][:, hf * 2048:(hf + 1) * 2048], in_=wst_sb[:]).then_inc(v_cst, 1)
                if j < NJX:
                    e.wait_ge(s_evx, 4 * j + 4)
                    e.wait_ge(s_hl, j + 1)
                    if j >= 1:
                        e.wait_ge(s_xc, j)
                        e.wait_ge(g_xc, 16 * j)
                    wc = lambda k: convw_sb[:, j * 4 + k:j * 4 + k + 1]
                    dve_sync(e, e.tensor_scalar(out=acc_sb[:], in0=xpre_sb[:, 0:T], scalar1=wc(0),
                                                scalar2=convb_sb[:, j:j + 1], op0=ALU.mult, op1=ALU.add))
                    dve_sync(e, e.scalar_tensor_tensor(out=xct_sb[:], in0=xpre_sb[:, 1:1 + T],
                                                       scalar=wc(1), in1=acc_sb[:], op0=ALU.mult, op1=ALU.add))
                    dve_sync(e, e.scalar_tensor_tensor(out=acc_sb[:], in0=xpre_sb[:, 2:2 + T],
                                                       scalar=wc(2), in1=xct_sb[:], op0=ALU.mult, op1=ALU.add))
                    e.scalar_tensor_tensor(out=xct_sb[:], in0=xpre_sb[:, 3:3 + T],
                                           scalar=wc(3), in1=acc_sb[:], op0=ALU.mult, op1=ALU.add).then_inc(v_cv, 1)

        def st2_tensor(e):
            for j in range(NJ):
                e.wait_ge(v_cst, 2 * j + 2)
                if j < NJX:
                    if j >= 1:
                        e.wait_ge(s_hl, j)
                    for ht in range(NH):
                        mm = e.matmul(pA[:, 0:3], wbf_sb[j % 2][:, ht * 128:(ht + 1) * 128],
                                      xnh_bf[:, ht * 3:(ht + 1) * 3],
                                      start=(ht == 0), stop=(ht == NH - 1))
                    mm.then_inc(t_hl, 1)
                for k in range(4):
                    e.wait_ge(s_evx, max(0, 4 * j + k - 3))
                    for ht in range(NH):
                        mm = e.matmul(pC[:, k * 512:(k + 1) * 512],
                                      wbf_sb[j % 2][:, ht * 128:(ht + 1) * 128],
                                      xn_rhs(xn_bf, ht, k),
                                      start=(ht == 0), stop=(ht == NH - 1))
                    mm.then_inc(t_mm, 1)

        def st2_scalar(e):
            for j in range(NJ):
                if j < NJX:
                    e.wait_ge(t_hl, j + 1)
                    if j >= 1:
                        e.wait_ge(v_cv, j)
                    e.activation(out=xpre_sb[:, 0:3], in_=pA[:, 0:3], func=AF.Copy).then_inc(s_hl, 1)
                    for k in range(4):
                        e.wait_ge(t_mm, 4 * j + k + 1)
                        e.activation(out=xpre_sb[:, 3 + k * 512:3 + (k + 1) * 512],
                                     in_=pC[:, k * 512:(k + 1) * 512], func=AF.Copy).then_inc(s_evx, 1)
                    e.wait_ge(v_cv, j + 1)
                    e.activation(out=acc_sb[:], in_=xct_sb[:], func=AF.Silu).then_inc(s_xc, 1)
                else:
                    jz = j - NJX
                    if jz >= 1:
                        e.wait_ge(g_sz, 16 * jz)
                    for k in range(4):
                        e.wait_ge(t_mm, 4 * j + k + 1)
                        e.activation(out=szt_sb[:, k * 512:(k + 1) * 512],
                                     in_=pC[:, k * 512:(k + 1) * 512], func=AF.Silu).then_inc(s_evx, 1)

        def st2_gpsimd(e):
            for j in range(NJX):
                e.wait_ge(s_xc, j + 1)
                e.dma_start(out=xc_sp[j * 128:(j + 1) * 128, :], in_=acc_sb[:]).then_inc(g_xc, 16)
            for jz in range(NJX):
                e.wait_ge(s_evx, 4 * (NJX + jz) + 4)
                e.dma_start(out=sz_sp[jz * 128:(jz + 1) * 128, :], in_=szt_sb[:]).then_inc(g_sz, 16)

        def st3_sync(e):
            for o in range(NO):
                if o >= 1:
                    e.wait_ge(v3_cst, o)
                e.dma_start(out=wst3_sb.ap().rearrange("p (ht o) -> p ht o", ht=NH),
                            in_=fw_in[:, o * 128:(o + 1) * 128].rearrange("(ht p) o -> p ht o", p=128)
                            ).then_inc(w3_ld, 16)

        def st3_vector(e):
            for o in range(NO):
                e.wait_ge(w3_ld, 16 * (o + 1))
                if o >= 2:
                    e.wait_ge(t3_mm, 4 * (o - 1))
                e.tensor_copy(out=wbf3_sb[o % 2][:], in_=wst3_sb[:]).then_inc(v3_cst, 1)

        def st3_tensor(e):
            for o in range(NO):
                e.wait_ge(v3_cst, o + 1)
                for k in range(4):
                    e.wait_ge(s3_ev, max(0, 4 * o + k - 3))
                    for ht in range(NH):
                        mm = e.matmul(pC[:, k * 512:(k + 1) * 512],
                                      wbf3_sb[o % 2][:, ht * 128:(ht + 1) * 128],
                                      xn_rhs(xn_bf, ht, k),
                                      start=(ht == 0), stop=(ht == NH - 1))
                    mm.then_inc(t3_mm, 1)

        def st3_scalar(e):
            for o in range(NO):
                if o >= 1:
                    e.wait_ge(g_dn, 16 * o)
                for k in range(4):
                    e.wait_ge(t3_mm, 4 * o + k + 1)
                    e.activation(out=dnt_sb[:, k * 512:(k + 1) * 512],
                                 in_=pC[:, k * 512:(k + 1) * 512], func=AF.Identity,
                                 bias=fbias_sb[:, o:o + 1]).then_inc(s3_ev, 1)

        def st3_gpsimd(e):
            for o in range(NO):
                e.wait_ge(s3_ev, 4 * o + 4)
                e.dma_start(out=dn_sp[o * 128:(o + 1) * 128, :], in_=dnt_sb[:]).then_inc(g_dn, 16)

        DTWQ = 4

        def st4_sync(e):
            e.wait_ge(g_xc, 16 * NJX)
            for q in range(DTWQ):
                if q >= 1:
                    e.wait_ge(v_xc2, q)
                rt, half = q // 2, q % 2
                e.dma_start(out=dtw_st_sb[:],
                            in_=dtw_in[rt * 128:(rt + 1) * 128, half * 4096:(half + 1) * 4096]
                            ).then_inc(x2_ld, 16)
            for c in range(NCH):
                for kt in range(ND):
                    i = c * ND + kt
                    if i >= 2:
                        e.wait_ge(v_xc2, DTWQ + 2 * (i - 1))
                    psem = x2e if i % 2 == 0 else x2o
                    e.dma_start(out=xcb_sb[i % 2][:],
                                in_=xc_sp[kt * 128:(kt + 1) * 128, c * LC:(c + 1) * LC]
                                ).then_inc(psem, 16)
                    e.dma_start(out=xpw_st_sb[i % 2][:],
                                in_=xpw_in[kt * 128:(kt + 1) * 128, :]).then_inc(psem, 16)

        def st4_vector(e):
            for q in range(DTWQ):
                e.wait_ge(x2_ld, 16 * (q + 1))
                e.tensor_copy(out=dtw_bf[:, q * 4096:(q + 1) * 4096], in_=dtw_st_sb[:]).then_inc(v_xc2, 1)
            for c in range(NCH):
                for kt in range(ND):
                    i = c * ND + kt
                    e.wait_ge(x2e if i % 2 == 0 else x2o, 32 * (i // 2 + 1))
                    if i >= 2:
                        e.wait_ge(t_sp, i - 1)
                    e.tensor_copy(out=xcbf_sb[i % 2][:], in_=xcb_sb[i % 2][:]).then_inc(v_xc2, 1)
                    e.tensor_copy(out=xpwbf_sb[i % 2][:], in_=xpw_st_sb[i % 2][:]).then_inc(v_xc2, 1)
                base = c * NORM_STEPS
                e.wait_ge(nseq, base + 7)
                e.scalar_tensor_tensor(out=dtrn_bf[:, 0:LC], in0=pA[:, 0:512], scalar=dtln_sb[:, 0:1],
                                       in1=rb_sb[:], op0=ALU.mult, op1=ALU.mult).then_inc(nseq, 1)
                e.scalar_tensor_tensor(out=dtrn_bf[:, LC:2 * LC], in0=pA[:, 512:1024], scalar=dtln_sb[:, 1:2],
                                       in1=rb_sb[:], op0=ALU.mult, op1=ALU.mult).then_inc(nseq, 1)
                e.wait_ge(nseq, base + 15)
                e.scalar_tensor_tensor(out=bv_sb[0:16, c * LC:(c + 1) * LC], in0=pB[0:16, 0:512],
                                       scalar=blnw_sb[:], in1=rbn_sb[0:16, 0:512],
                                       op0=ALU.mult, op1=ALU.mult).then_inc(nseq, 1)
                e.wait_ge(nseq, base + 22)
                e.scalar_tensor_tensor(out=cv_sb[0:16, c * LC:(c + 1) * LC], in0=pB[0:16, 512:1024],
                                       scalar=clnw_sb[:], in1=rbn_sb[0:16, 512:1024],
                                       op0=ALU.mult, op1=ALU.mult).then_inc(nseq, 1)
                for d in range(ND):
                    i = c * ND + d
                    e.wait_ge(s_sp, 2 * (i + 1))
                    if i >= 2:
                        e.wait_ge(g_dt, 16 * (i - 1))
                    e.tensor_add(out=dts_sb[i % 2][:], in0=spa_sb[:], in1=spb_sb[:]).then_inc(v_dt, 1)
                    e.wait_ge(v_dt, i + 1)
                    dve_sync(e, e.reduce_sum(out=col_sb[:, 0:1], in_=dts_sb[i % 2][:], axis=AX))
                    if c == 0:
                        e.tensor_copy(out=cdtot_sb[:, d:d + 1], in_=col_sb[:, 0:1])
                    else:
                        e.tensor_add(out=cdtot_sb[:, d:d + 1], in0=cdtot_sb[:, d:d + 1],
                                     in1=col_sb[:, 0:1])
            dve_sync(e, e.tensor_copy(out=col_sb[:, 1:2], in_=cdtot_sb[:, ND - 1:ND]))
            for d in range(ND):
                e.tensor_scalar(out=pcb_sb[:, d * NST:(d + 1) * NST],
                                in0=aexp_sb[:, d * NST:(d + 1) * NST],
                                scalar1=cdtot_sb[:, d:d + 1], scalar2=None, op0=ALU.mult).then_inc(v_pc, 1)

        def st4_tensor(e):
            for c in range(NCH):
                if c >= 1:
                    e.wait_ge(nseq, c * NORM_STEPS)
                for kt in range(ND):
                    i = c * ND + kt
                    e.wait_ge(v_xc2, DTWQ + 2 * (i + 1))
                    st, sp = (kt == 0), (kt == ND - 1)
                    e.matmul(pA[:, 0:512], xpwbf_sb[i % 2][:, 0:128], xcbf_sb[i % 2][:], start=st, stop=sp)
                    e.matmul(pA[:, 512:1024], xpwbf_sb[i % 2][:, 128:256], xcbf_sb[i % 2][:], start=st, stop=sp)
                    e.matmul(pB[0:16, 0:512], xpwbf_sb[i % 2][:, 256:272], xcbf_sb[i % 2][:], start=st, stop=sp)
                    e.matmul(pB[0:16, 512:1024], xpwbf_sb[i % 2][:, 272:288], xcbf_sb[i % 2][:], start=st, stop=sp).then_inc(t_sp, 1)
                base = c * NORM_STEPS
                e.wait_ge(nseq, base + 2)
                e.matmul(pC[0:1, 0:512], onesc_sb[:], sq_sb[:, 0:512], start=True, stop=False)
                e.matmul(pC[0:1, 0:512], onesc_sb[:], sq_sb[:, 512:1024], start=False, stop=True).then_inc(nseq, 1)
                e.wait_ge(nseq, base + 5)
                e.matmul(pC[:, 512:1024], onesr_sb[:], row_sb[0:1, 512:1024], start=True, stop=True).then_inc(nseq, 1)
                e.wait_ge(nseq, base + 10)
                e.matmul(pC[0:1, 0:512], ones16c_sb[:], sqn_sb[0:16, 0:512], start=True, stop=True).then_inc(nseq, 1)
                e.wait_ge(nseq, base + 13)
                e.matmul(pC[0:16, 512:1024], onesr_sb[0:1, 0:16], row_sb[0:1, 512:1024], start=True, stop=True).then_inc(nseq, 1)
                e.wait_ge(nseq, base + 17)
                e.matmul(pC[0:1, 0:512], ones16c_sb[:], sqn_sb[0:16, 512:1024], start=True, stop=True).then_inc(nseq, 1)
                e.wait_ge(nseq, base + 20)
                e.matmul(pC[0:16, 512:1024], onesr_sb[0:1, 0:16], row_sb[0:1, 512:1024], start=True, stop=True).then_inc(nseq, 1)
                e.wait_ge(nseq, base + 9)
                for d in range(ND):
                    i = c * ND + d
                    if i >= 2:
                        e.wait_ge(s_sp, 2 * (i - 1))
                    dst = pC[:, 1024 + (i % 2) * 512:1024 + (i % 2) * 512 + 512]
                    e.matmul(dst, dtw_bf[:, d * 128:d * 128 + 128], dtrn_bf[:, 0:LC],
                             start=True, stop=False)
                    e.matmul(dst, dtw_bf[:, DIN + d * 128:DIN + d * 128 + 128], dtrn_bf[:, LC:2 * LC],
                             start=False, stop=True).then_inc(t_dt, 1)

        def st4_scalar(e):
            for c in range(NCH):
                base = c * NORM_STEPS
                e.wait_ge(t_sp, (c + 1) * ND)
                e.activation(out=sq_sb[:, 0:512], in_=pA[:, 0:512], func=AF.Square).then_inc(nseq, 1)
                e.activation(out=sq_sb[:, 512:1024], in_=pA[:, 512:1024], func=AF.Square).then_inc(nseq, 1)
                e.wait_ge(nseq, base + 3)
                e.activation(out=row_sb[0:1, 0:512], in_=pC[0:1, 0:512], func=AF.Ln,
                             scale=1.0 / RNK, bias=EPS_RMS).then_inc(nseq, 1)
                e.wait_ge(nseq, base + 4)
                e.activation(out=row_sb[0:1, 512:1024], in_=row_sb[0:1, 0:512], func=AF.Exp, scale=-0.5).then_inc(nseq, 1)
                e.wait_ge(nseq, base + 6)
                e.activation(out=rb_sb[:], in_=pC[:, 512:1024], func=AF.Copy).then_inc(nseq, 1)
                e.wait_ge(nseq, base + 9)
                e.activation(out=sqn_sb[0:16, 0:512], in_=pB[0:16, 0:512], func=AF.Square).then_inc(nseq, 1)
                e.wait_ge(nseq, base + 11)
                e.activation(out=row_sb[0:1, 0:512], in_=pC[0:1, 0:512], func=AF.Ln,
                             scale=1.0 / NST, bias=EPS_RMS).then_inc(nseq, 1)
                e.wait_ge(nseq, base + 12)
                e.activation(out=row_sb[0:1, 512:1024], in_=row_sb[0:1, 0:512], func=AF.Exp, scale=-0.5).then_inc(nseq, 1)
                e.wait_ge(nseq, base + 14)
                e.activation(out=rbn_sb[0:16, 0:512], in_=pC[0:16, 512:1024], func=AF.Copy).then_inc(nseq, 1)
                e.wait_ge(nseq, base + 16)
                e.activation(out=sqn_sb[0:16, 512:1024], in_=pB[0:16, 512:1024], func=AF.Square).then_inc(nseq, 1)
                e.wait_ge(nseq, base + 18)
                e.activation(out=row_sb[0:1, 0:512], in_=pC[0:1, 0:512], func=AF.Ln,
                             scale=1.0 / NST, bias=EPS_RMS).then_inc(nseq, 1)
                e.wait_ge(nseq, base + 19)
                e.activation(out=row_sb[0:1, 512:1024], in_=row_sb[0:1, 0:512], func=AF.Exp, scale=-0.5).then_inc(nseq, 1)
                e.wait_ge(nseq, base + 21)
                e.activation(out=rbn_sb[0:16, 512:1024], in_=pC[0:16, 512:1024], func=AF.Copy).then_inc(nseq, 1)
                for d in range(ND):
                    i = c * ND + d
                    e.wait_ge(t_dt, i + 1)
                    if i >= 1:
                        e.wait_ge(v_dt, i)
                    src = pC[:, 1024 + (i % 2) * 512:1024 + (i % 2) * 512 + 512]
                    act_sync(e, e.activation(out=spa_sb[:], in_=src, func=AF.Abs,
                                             bias=dtbias_sb[:, d:d + 1]))
                    act_sync(e, e.activation(out=spb_sb[:], in_=spa_sb[:], func=AF.Exp, scale=-1.0))
                    e.activation(out=spa_sb[:], in_=spb_sb[:], func=AF.Ln, bias=1.0).then_inc(s_sp, 1)
                    e.activation(out=spb_sb[:], in_=src, func=AF.Relu,
                                 bias=dtbias_sb[:, d:d + 1]).then_inc(s_sp, 1)

        def st4_gpsimd(e):
            for c in range(NCH):
                for d in range(ND):
                    i = c * ND + d
                    e.wait_ge(v_dt, i + 1)
                    e.dma_start(out=dt_sp[d * 128:(d + 1) * 128, c * LC:(c + 1) * LC],
                                in_=dts_sb[i % 2][:]).then_inc(g_dt, 16)
            e.wait_ge(nseq, NCH * NORM_STEPS)
            e.dma_start(out=cv_out, in_=cv_sb[:]).then_inc(g_fin, 16)

        def st5_sync(e):
            e.wait_ge(g_dt, 16 * NCH * ND)
            e.dma_start(out=sel16_sb[:], in_=sel16_in).then_inc(p1_ld, 16)
            for c in range(NCH):
                for d in range(ND):
                    i = c * ND + d
                    if i >= 2:
                        e.wait_ge(s_da, (i - 1) * NST)
                        e.wait_ge(v_sc, (i - 1) * NST)
                    psem = p1e if i % 2 == 0 else p1o
                    e.dma_start(out=p1dt_sb[i % 2][:],
                                in_=dt_sp[d * 128:(d + 1) * 128, c * LC:(c + 1) * LC]).then_inc(psem, 16)
                    e.dma_start(out=p1xc_sb[i % 2][:],
                                in_=xc_sp[d * 128:(d + 1) * 128, c * LC:(c + 1) * LC]).then_inc(psem, 16)

        def st5_tensor(e):
            for c in range(NCH):
                if c >= 1:
                    e.wait_ge(v_sc, c * ND * NST)
                if c == 0:
                    e.wait_ge(p1_ld, 16)
                for n in range(NST):
                    for which in range(2):
                        m = c * 32 + 2 * n + which
                        if m >= 2:
                            e.wait_ge(s_br, m - 1)
                        src = bv_sb if which == 0 else cv_sb
                        e.matmul(pA[:, (m % 2) * 512:(m % 2) * 512 + 512],
                                 sel16_sb[:, n * 128:(n + 1) * 128],
                                 src[0:16, c * LC:(c + 1) * LC], start=True, stop=True).then_inc(t_br, 1)

        def st5_scalar(e):
            for c in range(NCH):
                for n in range(NST):
                    for which in range(2):
                        m = c * 32 + 2 * n + which
                        e.wait_ge(t_br, m + 1)
                        if c >= 1:
                            e.wait_ge(v_sc, c * ND * NST)
                        dst = brep_sb if which == 0 else crep_sb
                        e.activation(out=dst[:, n * LC:(n + 1) * LC],
                                     in_=pA[:, (m % 2) * 512:(m % 2) * 512 + 512], func=AF.Copy).then_inc(s_br, 1)
                for d in range(ND):
                    i = c * ND + d
                    e.wait_ge(p1e if i % 2 == 0 else p1o, 32 * (i // 2 + 1))
                    for n in range(NST):
                        gidx = i * NST + n
                        if gidx >= 2:
                            e.wait_ge(v_sc, gidx - 1)
                        e.activation(out=da_sb[gidx % 2][:], in_=p1dt_sb[i % 2][:], func=AF.Exp,
                                     scale=an_sb[:, d * NST + n:d * NST + n + 1]).then_inc(s_da, 1)
            e.wait_ge(v_pc, ND)
            e.activation(out=pce_sb[:], in_=pcb_sb[:], func=AF.Exp, scale=-1.0).then_inc(s_pc, 1)

        def st5_vector(e):
            for c in range(NCH):
                for d in range(ND):
                    i = c * ND + d
                    e.wait_ge(p1e if i % 2 == 0 else p1o, 32 * (i // 2 + 1))
                    e.tensor_mul(out=dtx_sb[:], in0=p1dt_sb[i % 2][:], in1=p1xc_sb[i % 2][:])
                    for n in range(NST):
                        gidx = i * NST + n
                        e.wait_ge(s_da, gidx + 1)
                        e.wait_ge(s_br, c * 32 + 2 * n + 2)
                        e.tensor_mul(out=u_sb[:], in0=dtx_sb[:], in1=brep_sb[:, n * LC:(n + 1) * LC])
                        e.tensor_tensor_scan(out=h_sb[:], data0=da_sb[gidx % 2][:], data1=u_sb[:],
                                             initial=state_sb[:, d * NST + n:d * NST + n + 1],
                                             op0=ALU.mult, op1=ALU.add).then_inc(v_sc, 1)
                        if n == 0:
                            if i >= 2:
                                e.wait_ge(g_y0, 16 * (i - 1))
                            e.scalar_tensor_tensor(out=y0a_sb[i % 2][:], in0=h_sb[:], scalar=1.0,
                                                   in1=crep_sb[:, 0:LC], op0=ALU.mult, op1=ALU.mult)
                        else:
                            e.scalar_tensor_tensor(out=ych_sb[:], in0=h_sb[:], scalar=1.0,
                                                   in1=crep_sb[:, n * LC:(n + 1) * LC],
                                                   op0=ALU.mult, op1=ALU.mult)
                            ad = e.tensor_add(out=y0a_sb[i % 2][:], in0=y0a_sb[i % 2][:], in1=ych_sb[:])
                            if n == NST - 1:
                                ad.then_inc(v_y0, 1)
                        e.tensor_copy(out=state_sb[:, d * NST + n:d * NST + n + 1],
                                      in_=h_sb[:, LC - 1:LC]).then_inc(v_st, 1)

        def st5_gpsimd(e):
            for c in range(NCH):
                for d in range(ND):
                    i = c * ND + d
                    e.wait_ge(v_y0, i + 1)
                    e.dma_start(out=y0_sp[d * 128:(d + 1) * 128, c * LC:(c + 1) * LC],
                                in_=y0a_sb[i % 2][:]).then_inc(g_y0, 16)
            e.wait_ge(v_st, NCH * ND * NST)
            e.dma_start(out=qc_out, in_=state_sb[:]).then_inc(g_fin, 16)
            e.wait_ge(s_pc, 1)
            e.dma_start(out=pc_out, in_=pce_sb[:]).then_inc(g_fin, 16)
            e.wait_ge(g_xc, 16 * NJX)
            e.wait_ge(g_sz, 16 * NJX)
            e.wait_ge(g_dn, 16 * NO)
            e.wait_ge(g_dt, 16 * NCH * ND)
            e.wait_ge(g_y0, 16 * NCH * ND)
            e.wait_ge(g_fin, 16 * (6 if debug_xn else 5))

        with contextlib.ExitStack() as esA:
            sbA = lambda n, sh, dt=F32: esA.enter_context(nc.sbuf_tensor(n, sh, dt))
            alog_sb = sbA("alog_sb", [128, ND * NST])
            xn_bf = sbA("xn_bf", [128, NT * H], BF16)
            block.sync(st0_sync)
            block.scalar(st0_scalar)
            block.vector(st0_vector)
            barrier()
            with contextlib.ExitStack() as es1:
                sb1 = lambda n, sh, dt=F32: es1.enter_context(nc.sbuf_tensor(n, sh, dt))
                cent_sb = sb1("cent_sb", [128, NH * 64])
                xb_sb = sb1("xb_sb", [128, H]); xw_sb = sb1("xw_sb", [128, H])
                xnf_sb = [sb1("xnf0", [128, 1024]), sb1("xnf1", [128, 1024])]
                e_sb = sb1("e_sb", [128, 64]); dots_sb = sb1("dots_sb", [128, 64])
                lgs_sb = sb1("lgs_sb", [128, 64])
                hits_sb = [sb1("hits0", [128, 64]), sb1("hits1", [128, 64])]
                block.sync(st1_sync)
                block.vector(st1_vector)
                block.scalar(st1_scalar)
                block.tensor(st1_tensor)
                block.scalar(st1_scalar_tail)
                block.gpsimd(st1_gpsimd)
                barrier()
            with contextlib.ExitStack() as es2:
                sb2 = lambda n, sh, dt=F32: es2.enter_context(nc.sbuf_tensor(n, sh, dt))
                wst_sb = sb2("wst_sb", [128, NH * 64])
                wbf_sb = [sb2("wbf0", [128, NH * 128], BF16), sb2("wbf1", [128, NH * 128], BF16)]
                xpre_sb = sb2("xpre_sb", [128, T + 3])
                acc_sb = sb2("acc_sb", [128, T])
                xct_sb = sb2("xct_sb", [128, T]); szt_sb = sb2("szt_sb", [128, T])
                block.sync(st2_sync)
                block.vector(st2_vector)
                block.tensor(st2_tensor)
                block.scalar(st2_scalar)
                block.gpsimd(st2_gpsimd)
                barrier()
            with contextlib.ExitStack() as es3:
                sb3 = lambda n, sh, dt=F32: es3.enter_context(nc.sbuf_tensor(n, sh, dt))
                wst3_sb = sb3("wst3_sb", [128, NH * 128])
                wbf3_sb = [sb3("wbf30", [128, NH * 128], BF16), sb3("wbf31", [128, NH * 128], BF16)]
                dnt_sb = sb3("dnt_sb", [128, T])
                block.sync(st3_sync)
                block.vector(st3_vector)
                block.tensor(st3_tensor)
                block.scalar(st3_scalar)
                block.gpsimd(st3_gpsimd)
                barrier()
        with contextlib.ExitStack() as es4:
            sb4 = lambda n, sh, dt=F32: es4.enter_context(nc.sbuf_tensor(n, sh, dt))
            bv_sb = sb4("bv_sb", [NST, T]); cv_sb = sb4("cv_sb", [NST, T])
            pcb_sb = sb4("pcb_sb", [128, ND * NST])
            es4b = es4.enter_context(contextlib.ExitStack())
            sb4 = lambda n, sh, dt=F32: es4b.enter_context(nc.sbuf_tensor(n, sh, dt))
            dtw_bf = sb4("dtw_bf", [128, 2 * DIN], BF16)
            dtw_st_sb = sb4("dtw_st", [128, 4096])
            xcb_sb = [sb4("xcb0", [128, LC]), sb4("xcb1", [128, LC])]
            xcbf_sb = [sb4("xcbf0", [128, LC], BF16), sb4("xcbf1", [128, LC], BF16)]
            xpw_st_sb = [sb4("xpwst0", [128, ECNT]), sb4("xpwst1", [128, ECNT])]
            xpwbf_sb = [sb4("xpwbf0", [128, ECNT], BF16), sb4("xpwbf1", [128, ECNT], BF16)]
            sq_sb = sb4("sq_sb", [128, 1024]); sqn_sb = sb4("sqn_sb", [NST, 1024])
            row_sb = sb4("row_sb", [1, 1024]); rb_sb = sb4("rb_sb", [128, 512])
            rbn_sb = sb4("rbn_sb", [NST, 1024])
            dtrn_bf = sb4("dtrn_bf", [128, 2 * LC], BF16)
            spa_sb = sb4("spa_sb", [128, LC]); spb_sb = sb4("spb_sb", [128, LC])
            dts_sb = [sb4("dts0", [128, LC]), sb4("dts1", [128, LC])]
            block.sync(st4_sync)
            block.vector(st4_vector)
            block.tensor(st4_tensor)
            block.scalar(st4_scalar)
            block.gpsimd(st4_gpsimd)
            barrier()
            es4b.close()
            with contextlib.ExitStack() as es5:
                sb5 = lambda n, sh, dt=F32: es5.enter_context(nc.sbuf_tensor(n, sh, dt))
                sel16_sb = sb5("sel16_sb", [NST, NST * 128])
                brep_sb = sb5("brep_sb", [128, NST * LC])
                crep_sb = sb5("crep_sb", [128, NST * LC])
                p1dt_sb = [sb5("p1dt0", [128, LC]), sb5("p1dt1", [128, LC])]
                p1xc_sb = [sb5("p1xc0", [128, LC]), sb5("p1xc1", [128, LC])]
                da_sb = [sb5("da0", [128, LC]), sb5("da1", [128, LC])]
                dtx_sb = sb5("dtx_sb", [128, LC]); u_sb = sb5("u_sb", [128, LC])
                h_sb = sb5("h_sb", [128, LC]); ych_sb = sb5("ych_sb", [128, LC])
                y0a_sb = [sb5("y0a0", [128, LC]), sb5("y0a1", [128, LC])]
                pce_sb = sb5("pce_sb", [128, ND * NST])
                block.sync(st5_sync)
                block.tensor(st5_tensor)
                block.scalar(st5_scalar)
                block.vector(st5_vector)
                block.gpsimd(st5_gpsimd)
    return nc



def build_l2():
    nc = bass.Bass("TRN2", target_bir_lowering=False, detect_race_conditions=False)

    y0_in = nc.dram_tensor("y0", [DIN, T], F32, kind="ExternalInput").ap()
    xc_in = nc.dram_tensor("xc", [DIN, T], F32, kind="ExternalInput").ap()
    dt_in = nc.dram_tensor("dt", [DIN, T], F32, kind="ExternalInput").ap()
    sz_in = nc.dram_tensor("sz", [DIN, T], F32, kind="ExternalInput").ap()
    dn_in = nc.dram_tensor("dn", [H, T], F32, kind="ExternalInput").ap()
    cv_in = nc.dram_tensor("cv", [NST, T], F32, kind="ExternalInput").ap()
    qst_in = nc.dram_tensor("qst", [NT, 128], F32, kind="ExternalInput").ap()
    hits_in = nc.dram_tensor("hitsall", [8, 64], F32, kind="ExternalInput").ap()
    qp_in = nc.dram_tensor("qp", [8, 128, ND * NST], F32, kind="ExternalInput").ap()
    pp_in = nc.dram_tensor("pp", [8, 128, ND * NST], F32, kind="ExternalInput").ap()
    x_in = nc.dram_tensor("x", [T, H], F32, kind="ExternalInput").ap()
    opw_in = nc.dram_tensor("opw", [DIN, H], F32, kind="ExternalInput").ap()
    alog_in = nc.dram_tensor("alog", [128, ND * NST], F32, kind="ExternalInput").ap()
    dpar_in = nc.dram_tensor("dpar", [128, ND], F32, kind="ExternalInput").ap()
    asn_in = nc.dram_tensor("asn", [1, H], I32, kind="ExternalInput").ap()
    iota_in = nc.dram_tensor("iota64", [64, 1], F32, kind="ExternalInput").ap()
    ident_in = nc.dram_tensor("ident", [128, 128], F32, kind="ExternalInput").ap()
    onesr_in = nc.dram_tensor("onesr", [1, 128], F32, kind="ExternalInput").ap()
    ones8_in = nc.dram_tensor("ones8", [8, 1], F32, kind="ExternalInput").ap()
    sel16_in = nc.dram_tensor("sel16", [NST, NST * 128], F32, kind="ExternalInput").ap()

    out_t = nc.dram_tensor("out", [T, H], F32, kind="ExternalOutput").ap()
    ybf_hbm = nc.dram_tensor("ybf", [DIN, T], BF16, kind="Internal").ap()

    NC2 = 12

    es = contextlib.ExitStack()
    with es:
        sb = lambda name, shape, dt=F32: es.enter_context(nc.sbuf_tensor(name, shape, dt))
        sem = lambda name: es.enter_context(nc.semaphore(name))

        aneg_sb = sb("aneg_sb", [128, ND * NST])
        aexp_sb = sb("aexp_sb", [128, ND * NST])
        dpar_sb = sb("dpar_sb", [128, ND])
        hin_sb = [sb("hin0", [128, ND * NST]), sb("hin1", [128, ND * NST])]
        qst_sb = sb("qst_sb", [NT, 128])
        qrep_sb = sb("qrep_sb", [128, T])
        rsc_sb = sb("rsc_sb", [128, 32])
        hits_sb = sb("hits_sb", [8, 64])
        clr_sb = sb("clr_sb", [1, 64])
        clc_sb = sb("clc_sb", [64, 1])
        iota_sb = sb("iota_sb", [64, 1])
        ident_sb = sb("ident_sb", [128, 128])
        onesr_sb = sb("onesr_sb", [1, 128])
        ones8_sb = sb("ones8_sb", [8, 1])
        sel16_sb = sb("sel16_sb", [NST, NST * 128])
        cdst_sb = sb("cdst_sb", [128, ND])
        ones_t = sb("ones_t", [128, LC])

        pA = es.enter_context(nc.psum_tensor("pA", [128, 1024], F32))
        pB = es.enter_context(nc.psum_tensor("pB", [128, 1024], F32))
        pC = es.enter_context(nc.psum_tensor("pC", [128, 2048], F32))

        c2_ld = sem("c2_ld"); stg = sem("stg"); useq = sem("useq")
        l1e = sem("l1e"); l1o = sem("l1o")
        t_cr = sem("t_cr"); s_cr = sem("s_cr")
        s_g = sem("s_g"); v_g = sem("v_g"); v_yb = sem("v_yb"); g_yb = sem("g_yb")
        y2_ld = sem("y2_ld"); x2_ld = sem("x2_ld"); w2_ld = sem("w2_ld"); v2c = sem("v2c")
        t2m = sem("t2m"); d2_ld = sem("d2_ld"); s2g = sem("s2g"); v2g = sem("v2g")
        t2t = sem("t2t"); v2a = sem("v2a"); g_out = sem("g_out")
        s_self = sem("s_self"); SSELF = [0]
        def act_sync(e, inst):
            SSELF[0] += 1
            inst.then_inc(s_self, 1)
            e.wait_ge(s_self, SSELF[0])
        v_self = sem("v_self"); VSELF = [0]
        def dve_sync(e, inst):
            VSELF[0] += 1
            inst.then_inc(v_self, 1)
            e.wait_ge(v_self, VSELF[0])

        block = es.enter_context(nc.Block())
        STG = [0]

        def barrier():
            STG[0] += 1
            k = STG[0]
            for emit in (block.sync, block.vector, block.scalar, block.tensor, block.gpsimd):
                def _f(e, k=k):
                    e.drain().then_inc(stg, 1)
                    e.wait_ge(stg, 5 * k)
                emit(_f)

        def u0_sync(e):
            e.dma_start(out=alog_sb[:], in_=alog_in).then_inc(c2_ld, 16)
            e.dma_start(out=dpar_sb[:], in_=dpar_in).then_inc(c2_ld, 16)
            e.dma_start(out=qst_sb[:], in_=qst_in).then_inc(c2_ld, 16)
            e.dma_start(out=hits_sb[:], in_=hits_in).then_inc(c2_ld, 16)
            e.dma_start(out=asn_i[:], in_=asn_in).then_inc(c2_ld, 16)
            e.dma_start(out=iota_sb[:], in_=iota_in).then_inc(c2_ld, 16)
            e.dma_start(out=ident_sb[:], in_=ident_in).then_inc(c2_ld, 16)
            e.dma_start(out=onesr_sb[:], in_=onesr_in).then_inc(c2_ld, 16)
            e.dma_start(out=ones8_sb[:], in_=ones8_in).then_inc(c2_ld, 16)
            e.dma_start(out=sel16_sb[:], in_=sel16_in).then_inc(c2_ld, 16)
            e.dma_start(out=qp_sb.ap().rearrange("p (b m) -> p b m", b=8),
                        in_=qp_in.rearrange("b p m -> p b m")).then_inc(c2_ld, 16)
            e.dma_start(out=pp_sb.ap().rearrange("p (b m) -> p b m", b=8),
                        in_=pp_in.rearrange("b p m -> p b m")).then_inc(c2_ld, 16)

        def u0_scalar(e):
            e.wait_ge(c2_ld, 16 * NC2)
            e.activation(out=aexp_sb[:], in_=alog_sb[:], func=AF.Exp).then_inc(useq, 1)
            e.wait_ge(useq, 4)
            e.activation(out=clr_sb[:], in_=pA[0:1, 0:64], func=AF.Copy).then_inc(useq, 1)
            e.wait_ge(useq, 7)
            e.activation(out=clc_sb[:], in_=pA[0:64, 64:65], func=AF.Copy).then_inc(useq, 1)
            for ch in range(8):
                e.wait_ge(useq, 8 + 4 * ch + 3)
                e.activation(out=rs_sb[0:1, ch * 512:(ch + 1) * 512], in_=pB[0:1, 0:512],
                             func=AF.Copy).then_inc(useq, 1)
            for tt in range(32):
                e.wait_ge(useq, 40 + 2 * tt + 1)
                e.activation(out=rsc_sb[:, tt:tt + 1], in_=pA[:, 128 + tt:129 + tt],
                             func=AF.Copy).then_inc(useq, 1)
            for tt in range(NT):
                e.wait_ge(useq, 104 + 2 * tt + 1)
                e.activation(out=qrep_sb[:, tt * 128:(tt + 1) * 128], in_=pC[:, 0:128],
                             func=AF.Copy).then_inc(useq, 1)

        def u0_vector(e):
            e.wait_ge(useq, 1)
            e.tensor_scalar(out=aneg_sb[:], in0=aexp_sb[:], scalar1=-1.0, scalar2=None,
                            op0=ALU.mult).then_inc(useq, 1)
            e.memset(ones_t[:], 1.0)
            e.memset(cdst_sb[:], 0.0)
            e.memset(hin_sb[0][:], 0.0).then_inc(useq, 1)
            dve_sync(e, e.tensor_copy(out=asn_f[:], in_=asn_i[:]))
            for b in range(8):
                src, dst = hin_sb[b % 2], hin_sb[(b + 1) % 2]
                dve_sync(e, e.tensor_mul(out=dst[:], in0=src[:],
                                         in1=pp_sb[:, b * ND * NST:(b + 1) * ND * NST]))
                dve_sync(e, e.tensor_add(out=dst[:], in0=dst[:],
                                         in1=qp_sb[:, b * ND * NST:(b + 1) * ND * NST]))
            e.wait_ge(useq, 5)
            dve_sync(e, e.tensor_scalar(out=clr_sb[:], in0=clr_sb[:], scalar1=0.0, scalar2=None,
                                        op0=ALU.is_gt))
            e.nop().then_inc(useq, 1)
            for ch in range(8):
                e.wait_ge(useq, 8 + 4 * ch + 1)
                e.tensor_scalar(out=eq_sb[:], in0=pB[0:64, 512:1024], scalar1=iota_sb[:, 0:1],
                                scalar2=None, op0=ALU.is_equal).then_inc(useq, 1)

        def u0_tensor(e):
            e.wait_ge(useq, 3)
            e.matmul(pA[0:1, 0:64], ones8_sb[:], hits_sb[:], start=True, stop=True).then_inc(useq, 1)
            e.wait_ge(useq, 6)
            e.transpose(out=pA[0:64, 64:65], in_=clr_sb[:], identity=ident_sb[0:1, 0:1]).then_inc(useq, 1)
            for ch in range(8):
                e.wait_ge(useq, 8 + 4 * ch)
                e.matmul(pB[0:64, 512:1024], onesr_sb[0:1, 0:64],
                         asn_f[0:1, ch * 512:(ch + 1) * 512], start=True, stop=True).then_inc(useq, 1)
                e.wait_ge(useq, 8 + 4 * ch + 2)
                e.matmul(pB[0:1, 0:512], clc_sb[:], eq_sb[:], start=True, stop=True).then_inc(useq, 1)
            for tt in range(32):
                e.wait_ge(useq, 40 + 2 * tt)
                e.transpose(out=pA[:, 128 + tt:129 + tt], in_=rs_sb[0:1, tt * 128:(tt + 1) * 128],
                            identity=ident_sb[0:1, 0:1]).then_inc(useq, 1)
            for tt in range(NT):
                e.wait_ge(useq, 104 + 2 * tt)
                e.matmul(pC[:, 0:128], sel16_sb[:, tt * 128:(tt + 1) * 128], qst_sb[:],
                         start=True, stop=True).then_inc(useq, 1)

        def u1_sync(e):
            e.dma_start(out=cv_sb[:], in_=cv_in).then_inc(c2_ld, 16)
            for c in range(NCH):
                for d in range(ND):
                    i = c * ND + d
                    if i >= 2:
                        e.wait_ge(v_yb, 2 * (i - 1))
                    ps = l1e if i % 2 == 0 else l1o
                    e.dma_start(out=u1dt[i % 2][:],
                                in_=dt_in[d * 128:(d + 1) * 128, c * LC:(c + 1) * LC]).then_inc(ps, 16)
                    e.dma_start(out=u1xc[i % 2][:],
                                in_=xc_in[d * 128:(d + 1) * 128, c * LC:(c + 1) * LC]).then_inc(ps, 16)
                    e.dma_start(out=u1sz[i % 2][:],
                                in_=sz_in[d * 128:(d + 1) * 128, c * LC:(c + 1) * LC]).then_inc(ps, 16)
                    e.dma_start(out=u1y0[i % 2][:],
                                in_=y0_in[d * 128:(d + 1) * 128, c * LC:(c + 1) * LC]).then_inc(ps, 16)

        def u1_tensor(e):
            e.wait_ge(c2_ld, 16 * (NC2 + 1))
            for c in range(NCH):
                if c >= 1:
                    e.wait_ge(v_g, c * ND * NST)
                for n in range(NST):
                    m = c * NST + n
                    if m >= 2:
                        e.wait_ge(s_cr, m - 1)
                    e.matmul(pA[:, (m % 2) * 512:(m % 2) * 512 + 512],
                             sel16_sb[:, n * 128:(n + 1) * 128],
                             cv_sb[0:16, c * LC:(c + 1) * LC], start=True, stop=True).then_inc(t_cr, 1)

        def u1_scalar(e):
            for c in range(NCH):
                for n in range(NST):
                    m = c * NST + n
                    e.wait_ge(t_cr, m + 1)
                    if c >= 1:
                        e.wait_ge(v_g, c * ND * NST)
                    e.activation(out=crep_sb[:, n * LC:(n + 1) * LC],
                                 in_=pA[:, (m % 2) * 512:(m % 2) * 512 + 512],
                                 func=AF.Copy).then_inc(s_cr, 1)
                for d in range(ND):
                    i = c * ND + d
                    e.wait_ge(v_yb, 2 * i + 1)
                    for n in range(NST):
                        gidx = i * NST + n
                        if gidx >= 2:
                            e.wait_ge(v_g, gidx - 1)
                        e.activation(out=gbuf[gidx % 2][:], in_=cdt_sb[i % 2][:], func=AF.Exp,
                                     scale=aneg_sb[:, d * NST + n:d * NST + n + 1]).then_inc(s_g, 1)

        def u1_vector(e):
            for c in range(NCH):
                for d in range(ND):
                    i = c * ND + d
                    e.wait_ge(l1e if i % 2 == 0 else l1o, 64 * (i // 2 + 1))
                    e.tensor_tensor_scan(out=cdt_sb[i % 2][:], data0=ones_t[:], data1=u1dt[i % 2][:],
                                         initial=cdst_sb[:, d:d + 1], op0=ALU.mult,
                                         op1=ALU.add).then_inc(v_yb, 1)
                    e.tensor_copy(out=cdst_sb[:, d:d + 1], in_=cdt_sb[i % 2][:, LC - 1:LC])
                    for n in range(NST):
                        gidx = i * NST + n
                        e.wait_ge(s_g, gidx + 1)
                        e.wait_ge(s_cr, c * NST + n + 1)
                        e.scalar_tensor_tensor(out=corr_sb[:], in0=gbuf[gidx % 2][:],
                                               scalar=hin_f[:, d * NST + n:d * NST + n + 1],
                                               in1=crep_sb[:, n * LC:(n + 1) * LC],
                                               op0=ALU.mult, op1=ALU.mult)
                        if n == 0:
                            e.tensor_add(out=yac_sb[:], in0=u1y0[i % 2][:], in1=corr_sb[:]).then_inc(v_g, 1)
                        else:
                            e.tensor_add(out=yac_sb[:], in0=yac_sb[:], in1=corr_sb[:]).then_inc(v_g, 1)
                    e.scalar_tensor_tensor(out=corr_sb[:], in0=u1xc[i % 2][:],
                                           scalar=dpar_sb[:, d:d + 1], in1=yac_sb[:],
                                           op0=ALU.mult, op1=ALU.add)
                    if i >= 2:
                        e.wait_ge(g_yb, 16 * (i - 1))
                    e.tensor_mul(out=ybf_sb[i % 2][:], in0=corr_sb[:],
                                 in1=u1sz[i % 2][:]).then_inc(v_yb, 1)

        def u1_gpsimd(e):
            for c in range(NCH):
                for d in range(ND):
                    i = c * ND + d
                    e.wait_ge(v_yb, 2 * (i + 1))
                    e.dma_start(out=ybf_hbm[d * 128:(d + 1) * 128, c * LC:(c + 1) * LC],
                                in_=ybf_sb[i % 2][:]).then_inc(g_yb, 16)

        def u2_sync(e):
            e.wait_ge(g_yb, 16 * NCH * ND)
            for q in range(4):
                if q >= 1:
                    e.wait_ge(t2m, 32 * q)
                e.dma_start(out=y2_sb.ap().rearrange("p (dt l) -> p dt l", dt=ND),
                            in_=ybf_hbm[:, q * 512:(q + 1) * 512]
                            .rearrange("(dt p) l -> p dt l", p=128)).then_inc(y2_ld, 16)
                if q >= 1:
                    e.wait_ge(g_out, 16 * 4 * q)
                for sub in range(4):
                    e.dma_start(out=otm_sb[:, sub * H:(sub + 1) * H],
                                in_=x_in[q * 512 + sub * 128:q * 512 + (sub + 1) * 128, :]
                                ).then_inc(x2_ld, 16)
                for k in range(NO):
                    kk = q * NO + k
                    for hf in range(2):
                        if 2 * kk + hf > 0:
                            e.wait_ge(v2c, 2 * kk + hf)
                        e.dma_start(out=w2st_sb.ap().rearrange("p (dt j) -> p dt j", dt=32),
                                    in_=opw_in[hf * 4096:(hf + 1) * 4096, k * 128:(k + 1) * 128]
                                    .rearrange("(dt p) j -> p dt j", p=128)).then_inc(w2_ld, 16)
                    dps = d2e if kk % 2 == 0 else d2o
                    if kk >= 2:
                        e.wait_ge(v2g, kk - 1)
                    e.dma_start(out=dn2_sb[kk % 2][:],
                                in_=dn_in[k * 128:(k + 1) * 128, q * 512:(q + 1) * 512]).then_inc(dps, 16)

        def u2_vector(e):
            for q in range(4):
                e.wait_ge(x2_ld, 16 * 4 * (q + 1))
                for k in range(NO):
                    kk = q * NO + k
                    for hf in range(2):
                        e.wait_ge(w2_ld, 16 * (2 * kk + hf + 1))
                        if kk >= 1 and hf == 0:
                            e.wait_ge(t2m, kk)
                        e.tensor_copy(out=w2bf_sb[:, hf * 4096:(hf + 1) * 4096],
                                      in_=w2st_sb[:]).then_inc(v2c, 1)
                    e.wait_ge(t2m, kk + 1)
                    e.wait_ge(s2g, kk + 1)
                    e.tensor_scalar(out=g2_sb[:], in0=g2_sb[:], scalar1=rsc_sb[:, k:k + 1],
                                    scalar2=None, op0=ALU.mult)
                    dve_sync(e, e.tensor_mul(out=g2_sb[:], in0=g2_sb[:],
                                             in1=qrep_sb[:, q * 512:(q + 1) * 512]))
                    e.tensor_mul(out=sg_sb[:], in0=pC[:, (kk % 2) * 512:(kk % 2) * 512 + 512],
                                 in1=g2_sb[:]).then_inc(v2g, 1)
                    for sub in range(4):
                        e.wait_ge(t2t, 4 * kk + 4)
                        ad = e.tensor_add(out=otm_sb[:, sub * H + k * 128:sub * H + (k + 1) * 128],
                                          in0=otm_sb[:, sub * H + k * 128:sub * H + (k + 1) * 128],
                                          in1=pB[:, sub * 128:(sub + 1) * 128])
                        if sub == 3:
                            ad.then_inc(v2a, 1)

        def u2_tensor(e):
            for q in range(4):
                e.wait_ge(y2_ld, 16 * (q + 1))
                for k in range(NO):
                    kk = q * NO + k
                    mm = None
                    for dt_i in range(ND):
                        if dt_i == 0:
                            e.wait_ge(v2c, 2 * kk + 1)
                            if kk >= 2:
                                e.wait_ge(v2g, kk - 1)
                        if dt_i == 32:
                            e.wait_ge(v2c, 2 * kk + 2)
                        mm = e.matmul(pC[:, (kk % 2) * 512:(kk % 2) * 512 + 512],
                                      w2bf_sb[:, dt_i * 128:(dt_i + 1) * 128],
                                      y2_sb[:, dt_i * 512:(dt_i + 1) * 512],
                                      start=(dt_i == 0), stop=(dt_i == ND - 1))
                    mm.then_inc(t2m, 1)
                    e.wait_ge(v2g, kk + 1)
                    if kk >= 1:
                        e.wait_ge(v2a, kk)
                    for sub in range(4):
                        e.transpose(out=pB[:, sub * 128:(sub + 1) * 128],
                                    in_=sg_sb[:, sub * 128:(sub + 1) * 128],
                                    identity=ident_sb[:]).then_inc(t2t, 1)

        def u2_scalar(e):
            for q in range(4):
                for k in range(NO):
                    kk = q * NO + k
                    e.wait_ge(d2e if kk % 2 == 0 else d2o, 16 * (kk // 2 + 1))
                    if kk >= 1:
                        e.wait_ge(v2g, kk)
                    e.activation(out=g2_sb[:], in_=dn2_sb[kk % 2][:], func=AF.Silu).then_inc(s2g, 1)

        def u2_gpsimd(e):
            for q in range(4):
                for sub in range(4):
                    e.wait_ge(v2a, 32 * (q + 1))
                    e.dma_start(out=out_t[q * 512 + sub * 128:q * 512 + (sub + 1) * 128, :],
                                in_=otm_sb[:, sub * H:(sub + 1) * H]).then_inc(g_out, 16)
            e.wait_ge(g_out, 16 * 16)

        with contextlib.ExitStack() as e0:
            sb0 = lambda n, sh, dt=F32: e0.enter_context(nc.sbuf_tensor(n, sh, dt))
            pp_sb = sb0("pp_sb", [128, 8 * ND * NST])
            qp_sb = sb0("qp_sb", [128, 8 * ND * NST])
            alog_sb = sb0("alog_sb", [128, ND * NST])
            asn_i = sb0("asn_i", [1, H], I32)
            asn_f = sb0("asn_f", [1, H])
            eq_sb = sb0("eq_sb", [64, 512])
            rs_sb = sb0("rs_sb", [1, H])
            block.sync(u0_sync)
            block.scalar(u0_scalar)
            block.vector(u0_vector)
            block.tensor(u0_tensor)
            barrier()
        hin_f = hin_sb[0]
        with contextlib.ExitStack() as e1:
            sb1 = lambda n, sh, dt=F32: e1.enter_context(nc.sbuf_tensor(n, sh, dt))
            cv_sb = sb1("cv_sb", [NST, T])
            u1dt = [sb1("u1dt0", [128, LC]), sb1("u1dt1", [128, LC])]
            u1xc = [sb1("u1xc0", [128, LC]), sb1("u1xc1", [128, LC])]
            u1sz = [sb1("u1sz0", [128, LC]), sb1("u1sz1", [128, LC])]
            u1y0 = [sb1("u1y00", [128, LC]), sb1("u1y01", [128, LC])]
            cdt_sb = [sb1("cdt0", [128, LC]), sb1("cdt1", [128, LC])]
            gbuf = [sb1("gb0", [128, LC]), sb1("gb1", [128, LC])]
            crep_sb = sb1("crep_sb", [128, NST * LC])
            corr_sb = sb1("corr_sb", [128, LC])
            yac_sb = sb1("yac_sb", [128, LC])
            ybf_sb = [sb1("ybf0", [128, LC], BF16), sb1("ybf1", [128, LC], BF16)]
            block.sync(u1_sync)
            block.tensor(u1_tensor)
            block.scalar(u1_scalar)
            block.vector(u1_vector)
            block.gpsimd(u1_gpsimd)
            barrier()
        with contextlib.ExitStack() as e2:
            sb2 = lambda n, sh, dt=F32: e2.enter_context(nc.sbuf_tensor(n, sh, dt))
            y2_sb = sb2("y2_sb", [128, ND * 512], BF16)
            otm_sb = sb2("otm_sb", [128, 4 * H])
            w2st_sb = sb2("w2st_sb", [128, 32 * 128])
            w2bf_sb = sb2("w2bf_sb", [128, ND * 128], BF16)
            dn2_sb = [sb2("dn20", [128, 512]), sb2("dn21", [128, 512])]
            g2_sb = sb2("g2_sb", [128, 512])
            sg_sb = sb2("sg_sb", [128, 512])
            d2e = sem("d2e"); d2o = sem("d2o")
            block.sync(u2_sync)
            block.vector(u2_vector)
            block.tensor(u2_tensor)
            block.scalar(u2_scalar)
            block.gpsimd(u2_gpsimd)
    return nc



def _l1_inputs(inp):
    x_full = np.ascontiguousarray(inp["L0_attn_hidden_states"].reshape(NCORE * T, H))
    wp = np.ascontiguousarray(inp["L0_attn_in_proj_w"])
    xpw = np.ascontiguousarray(inp["L0_attn_x_proj_w"].T)
    dtw = np.ascontiguousarray(inp["L0_attn_dt_proj_w"].T)
    fw = np.ascontiguousarray(inp["L0_ffn_weight"].T)
    cent = np.ascontiguousarray(inp["L0_ffn_centroids"].T)
    alog = np.ascontiguousarray(
        inp["L0_attn_A_log"].reshape(ND, 128, NST).transpose(1, 0, 2).reshape(128, ND * NST))
    convw = np.ascontiguousarray(
        inp["L0_attn_conv_w"].reshape(NJX, 128, 4).transpose(1, 0, 2).reshape(128, NJX * 4))
    convb = np.ascontiguousarray(inp["L0_attn_conv_b"].reshape(NJX, 128).T)
    dtbias = np.ascontiguousarray(inp["L0_attn_dt_bias"].reshape(ND, 128).T)
    dtln = np.ascontiguousarray(inp["L0_attn_dt_ln_w"].reshape(2, 128).T)
    blnw = np.ascontiguousarray(inp["L0_attn_B_ln_w"].reshape(NST, 1))
    clnw = np.ascontiguousarray(inp["L0_attn_C_ln_w"].reshape(NST, 1))
    fbias = np.ascontiguousarray(inp["L0_ffn_bias"].reshape(NO, 128).T)
    ident = np.eye(128, dtype=np.float32)
    onesc = np.ones((128, 1), np.float32)
    onesr = np.ones((1, 128), np.float32)
    ones16c = np.ones((NST, 1), np.float32)
    sel16 = np.zeros((NST, NST * 128), np.float32)
    for n in range(NST):
        sel16[n, n * 128:(n + 1) * 128] = 1.0
    fconst = np.zeros((128, 4), np.float32)
    fconst[:, 0] = EPS_LN
    fconst[:, 1] = EPS_RMS
    fconst[:, 2] = 1.0
    fconst[:, 3] = 10.0
    shared = dict(wp=wp, xpw=xpw, dtw=dtw, fw=fw, cent=cent, alog=alog, convw=convw,
                  convb=convb, dtbias=dtbias, dtln=dtln, blnw=blnw, clnw=clnw,
                  fbias=fbias, ident=ident, onesc=onesc, onesr=onesr,
                  ones16c=ones16c, sel16=sel16, fconst=fconst)
    maps = []
    for c in range(NCORE):
        xs = np.ascontiguousarray(x_full[c * T:(c + 1) * T])
        if c == 0:
            halo = np.zeros((3, H), np.float32)
        else:
            halo = np.ascontiguousarray(x_full[c * T - 3:c * T])
        maps.append(dict(shared, x=xs, xhalo=halo))
    return maps


def _l2_inputs(inp, res1):
    x_full = np.ascontiguousarray(inp["L0_attn_hidden_states"].reshape(NCORE * T, H))
    opw = np.ascontiguousarray(inp["L0_attn_out_proj_w"].T)
    alog = np.ascontiguousarray(
        inp["L0_attn_A_log"].reshape(ND, 128, NST).transpose(1, 0, 2).reshape(128, ND * NST))
    dpar = np.ascontiguousarray(inp["L0_attn_D_param"].reshape(ND, 128).T)
    asn = np.ascontiguousarray(inp["L0_ffn_assignments"].reshape(1, H).astype(np.int32))
    iota64 = np.arange(64, dtype=np.float32).reshape(64, 1)
    ident = np.eye(128, dtype=np.float32)
    onesr = np.ones((1, 128), np.float32)
    ones8 = np.ones((8, 1), np.float32)
    sel16 = np.zeros((NST, NST * 128), np.float32)
    for n in range(NST):
        sel16[n, n * 128:(n + 1) * 128] = 1.0
    hits_all = np.ascontiguousarray(
        np.stack([res1[c]["hits"][0] for c in range(NCORE)]).astype(np.float32))
    qcs = [res1[c]["qc"] for c in range(NCORE)]
    pcs = [res1[c]["pc"] for c in range(NCORE)]
    shared = dict(opw=opw, alog=alog, dpar=dpar, asn=asn, iota64=iota64, ident=ident,
                  onesr=onesr, ones8=ones8, sel16=sel16, hitsall=hits_all)
    maps = []
    zeros = np.zeros((128, ND * NST), np.float32)
    ones = np.ones((128, ND * NST), np.float32)
    for c in range(NCORE):
        qp = np.stack([qcs[b] if b < c else zeros for b in range(8)])
        pp = np.stack([pcs[b] if b < c else ones for b in range(8)])
        maps.append(dict(shared,
                         y0=res1[c]["y0_sp"], xc=res1[c]["xc_sp"], dt=res1[c]["dt_sp"],
                         sz=res1[c]["sz_sp"], dn=res1[c]["dn_sp"], cv=res1[c]["cv"],
                         qst=res1[c]["qselt"], qp=np.ascontiguousarray(qp),
                         pp=np.ascontiguousarray(pp),
                         x=np.ascontiguousarray(x_full[c * T:(c + 1) * T])))
    return maps


def kernel(**inputs):
    maps1 = _l1_inputs(inputs)
    nc1 = build_l1()
    res1 = run_bass_kernel_spmd(nc1, maps1, core_ids=list(range(NCORE)))
    maps2 = _l2_inputs(inputs, res1.results)
    nc2 = build_l2()
    res2 = run_bass_kernel_spmd(nc2, maps2, core_ids=list(range(NCORE)))
    out = np.concatenate([res2.results[c]["out"] for c in range(NCORE)], axis=0)
    return out.reshape(1, NCORE * T, H)
